# version 14
# baseline (speedup 1.0000x reference)
import os
import numpy as np
import ml_dtypes
from contextlib import ExitStack

import concourse.bass as bass
import concourse.mybir as mybir
from concourse.bass_utils import run_bass_kernel_spmd

F32 = mybir.dt.float32
BF16 = mybir.dt.bfloat16
I32 = mybir.dt.int32
U8 = mybir.dt.uint8
AF = mybir.ActivationFunctionType
ALU = mybir.AluOpType
AX = mybir.AxisListType
BF = ml_dtypes.bfloat16

DM = 4096
T = 2048
NT = 16
NC = 32
H = 32
KVH = 8
CH = 128
NCH = 16
W = 64
GN = 16.0
EPS = 1e-5
ISQD = float(1.0 / np.sqrt(128.0))

DEBUG = [x for x in os.environ.get("KDEBUG", "").split(",") if x]


class Prog:

    def __init__(self, resolve=None):
        self.nc = bass.Bass("TRN2", target_bir_lowering=False,
                            detect_race_conditions=os.environ.get("KRACE", "0") == "1")
        self.es = ExitStack()
        self.marks = {}
        self.chans = {}
        self.cnt = {}
        self.sems = {}
        self.resolve = resolve

    def _sem(self, ch):
        if ch not in self.sems:
            self.sems[ch] = self.es.enter_context(self.nc.semaphore(f"c_{ch}"))
        return self.sems[ch]

    def inc(self, inst, name, k=1):
        ins = inst.ins
        is_dma = "DMA" in type(ins).__name__ or "TensorLoad" in type(ins).__name__
        ch = f"{ins.engine}:{'d' if is_dma else 'c'}"
        if name in self.chans:
            assert self.chans[name] == ch, f"{name}: {self.chans[name]} vs {ch}"
        else:
            self.chans[name] = ch
        self.cnt[ch] = self.cnt.get(ch, 0) + k
        self.marks.setdefault(name, []).append(self.cnt[ch])
        inst.then_inc(self._sem(ch), k)
        return len(self.marks[name])

    def n(self, name):
        return len(self.marks.get(name, []))

    def w(self, eng, name, events):
        if events is None or events <= 0:
            return
        if self.resolve is None:
            return
        rmarks, rchans = self.resolve
        lst = rmarks.get(name)
        assert lst is not None, f"wait on never-inc'd sem {name}"
        assert events <= len(lst), f"wait {name}[{events}] > {len(lst)} incs"
        eng.wait_ge(self._sem(rchans[name]), lst[events - 1])

    def wall(self, eng, name):
        self.w(eng, name, self.n(name))

    def bar(self, eng, inst, name):
        self.inc(inst, name)
        self.w(eng, name, self.n(name))


class Arena:
    def __init__(self, p, name, nbytes):
        self.t = p.es.enter_context(p.nc.sbuf_tensor(name, [128, nbytes], U8))
        self.nbytes = nbytes
        self.off = 0

    def take(self, n_elems, dt):
        sz = {F32: 4, BF16: 2, I32: 4}[dt]
        nb = (n_elems * sz + 31) // 32 * 32
        assert self.off + nb <= self.nbytes, f"{self.t.name} overflow {self.off}+{nb}>{self.nbytes}"
        v = self.t.ap()[:, self.off:self.off + n_elems * sz].bitcast(dt)
        self.off += nb
        return v

    def mark(self):
        return self.off

    def reset(self, off=0):
        self.off = off


def build_program():
    p1 = _build(Prog(None))
    return _build(Prog((p1.marks, p1.chans)))


def _build(p):
    nc = p.nc
    es = p.es

    d_x = nc.dram_tensor("x_own", [T, DM], F32, kind="ExternalInput").ap()
    d_xp = nc.dram_tensor("x_pre", [T, DM], F32, kind="ExternalInput").ap()
    d_wq = nc.dram_tensor("wq", [DM, DM], F32, kind="ExternalInput").ap()
    d_wk = nc.dram_tensor("wk", [DM, 1024], F32, kind="ExternalInput").ap()
    d_wv = nc.dram_tensor("wv", [DM, 1024], F32, kind="ExternalInput").ap()
    d_wo = nc.dram_tensor("wo", [DM, DM], F32, kind="ExternalInput").ap()
    d_pk = nc.dram_tensor("pk", [DM, 512], I32, kind="ExternalInput").ap()
    d_sc = nc.dram_tensor("sc", [DM, 32], F32, kind="ExternalInput").ap()
    d_bias = nc.dram_tensor("fbias", [1, DM], F32, kind="ExternalInput").ap()
    d_t1 = nc.dram_tensor("t1", [128, 2112], BF16, kind="ExternalInput").ap()
    d_t2 = nc.dram_tensor("t2", [128, 2112], BF16, kind="ExternalInput").ap()
    d_t1s = nc.dram_tensor("t1s", [128, 2112], BF16, kind="ExternalInput").ap()
    d_t2s = nc.dram_tensor("t2s", [128, 2112], BF16, kind="ExternalInput").ap()
    d_m0 = nc.dram_tensor("swam0", [128, 192], F32, kind="ExternalInput").ap()
    d_mi = nc.dram_tensor("swami", [128, 192], F32, kind="ExternalInput").ap()
    d_tril = nc.dram_tensor("tril", [128, 128], F32, kind="ExternalInput").ap()
    d_id = nc.dram_tensor("ident", [128, 128], BF16, kind="ExternalInput").ap()
    d_ones = nc.dram_tensor("ones", [128, 128], BF16, kind="ExternalInput").ap()
    d_psig = nc.dram_tensor("psig", [128, 128], BF16, kind="ExternalInput").ap()
    d_flag = nc.dram_tensor("flag", [128, 2], F32, kind="ExternalInput").ap()

    d_out = nc.dram_tensor("out", [T, DM], F32, kind="ExternalOutput").ap()
    dbg_specs = {
        "dbg_S": ([128, KVH * 128], F32),
        "dbg_attn": ([128, H * 1024], BF16),
        "dbg_attn2": ([128, H * 1024], BF16),
    }
    d_dbg = {nm: nc.dram_tensor(nm, *dbg_specs[nm], kind="ExternalOutput").ap() for nm in DEBUG}

    skind = {"kind": "ExternalOutput"} if os.environ.get("KSCROUT") else {}
    s_h = nc.dram_tensor("h_dram", [T, DM], BF16, **skind).ap()
    s_qt = nc.dram_tensor("qt_dram", [H, 128, T], BF16, **skind).ap()
    s_u1 = nc.dram_tensor("u1_dram", [H, 128, T], BF16, **skind).ap()
    s_u2 = nc.dram_tensor("u2_dram", [H, 128, T], BF16, **skind).ap()
    s_kt = nc.dram_tensor("kt_dram", [128, KVH * T], BF16, **skind).ap()
    s_eB = nc.dram_tensor("eB_dram", [128, KVH * T], BF16, **skind).ap()
    s_kh = nc.dram_tensor("kh_dram", [128, KVH * T], BF16, **skind).ap()
    s_w12 = nc.dram_tensor("w12_dram", [128, KVH * 2112], BF16, **skind).ap()
    s_w34 = nc.dram_tensor("w34_dram", [128, KVH * 2112], BF16, **skind).ap()
    s_V = nc.dram_tensor("V_dram", [NT + 1, 128, 1024], BF16, **skind).ap()
    s_x2 = nc.dram_tensor("x2_dram", [T, DM], F32, **skind).ap()
    s_op = nc.dram_tensor("oproj_dram", [T, DM], BF16, **skind).ap()
    s_x2n = nc.dram_tensor("x2n_dram", [T, DM], BF16, **skind).ap()
    s_ffn = nc.dram_tensor("ffn_dram", [T, DM], BF16, **skind).ap()
    s_wdq = nc.dram_tensor("wdq_dram", [DM, DM], BF16, **skind).ap()

    A = Arena(p, "A", 131072)
    B = Arena(p, "B", 65536)
    C = Arena(p, "C", 15872)

    t_id = C.take(128, BF16)
    t_ones = C.take(128, BF16)
    t_psig = C.take(128, BF16)
    t_fl2 = C.take(2, F32)
    t_eps = t_fl2[:, 1:2]
    t_tril = C.take(128, F32)
    t_m0 = C.take(192, F32)
    t_mi = C.take(192, F32)
    t_biasb = C.take(512, BF16)
    t_S = C.take(KVH * 128, F32)
    t_Sb = C.take(KVH * 128, BF16)
    t_D = C.take(KVH * NCH, F32)
    t_R = C.take(16, F32)
    t_khalo = C.take(KVH * 64, BF16)
    t_st = C.take(352, F32)
    t_c1 = C.take(16, F32)
    
    t_fst = [C.take(512, BF16) for _ in range(2)]
    t_zz = C.take(2, F32)

    blk = es.enter_context(nc.Block())
    psum = [es.enter_context(nc.psum_tensor(f"pb{i}", [128, 512], F32)) for i in range(8)]
    psb = [q.ap().bitcast(BF16) for q in psum]

    def fol(eng, name):
        ins = eng.dma_start(out=t_zz, in_=d_flag)
        p.inc(ins, name, 16)

    t_T1 = B.take(2112, BF16)
    t_T2 = B.take(2112, BF16)
    t_T1s = B.take(2112, BF16)
    t_T2s = B.take(2112, BF16)
    B_TAB = B.mark()

    hT = A.t.ap()[:, 0:131072].bitcast(BF16)

    @blk.sync
    def _(sy):
        for dst, dsrc in [(t_tril, d_tril), (t_m0, d_m0), (t_mi, d_mi), (t_fl2, d_flag),
                          (t_id, d_id)]:
            ins = sy.dma_start(out=dst, in_=dsrc)
            p.inc(ins, "ldc", 16)

    @blk.gpsimd
    def _(g):
        for dst, dsrc in [(t_ones, d_ones), (t_psig, d_psig), (t_T1, d_t1),
                          (t_T2, d_t2), (t_T1s, d_t1s), (t_T2s, d_t2s)]:
            ins = g.dma_start(out=dst, in_=dsrc)
            p.inc(ins, "ldcg", 16)

    def p_ln1(xsrc, tag, awaits=()):
        a0 = A.mark()
        A.reset(0)
        xb = [A.take(DM, F32) for _ in range(2)]
        sq = A.take(DM, BF16)
        hst = [A.take(DM, BF16) for _ in range(2)]
        A.reset(a0)
        stS, stQ = t_st[:, 0:16], t_st[:, 16:32]
        stM, stR = t_st[:, 32:48], t_st[:, 48:64]
        stN, stV = t_st[:, 64:80], t_st[:, 80:96]

        @blk.sync
        def _(sy):
            for nm, v in awaits:
                p.w(sy, nm, v)
            for i in range(NT):
                if i >= 2:
                    p.w(sy, "hn" + tag, i - 1)
                ins = sy.dma_start(out=xb[i % 2], in_=xsrc[i * 128:(i + 1) * 128, :])
                p.inc(ins, "zd", 16)
                fol(sy, "xld" + tag)

        @blk.vector
        def _(v):
            for i in range(NT):
                p.w(v, "xld" + tag, ((i + 1)))
                ins = v.reduce_sum(stS[:, i:i + 1], xb[i % 2], AX.X)
                p.bar(v, ins, "zv")
                ins = v.tensor_scalar(stM[:, i:i + 1], stS[:, i:i + 1], 1.0 / DM, None, ALU.mult)
                p.bar(v, ins, "zv")
                ins = v.tensor_tensor(stV[:, i:i + 1], stM[:, i:i + 1], stM[:, i:i + 1], ALU.mult)
                p.bar(v, ins, "zv")
                p.w(v, "sq" + tag, i + 1)
                ins = v.scalar_tensor_tensor(stV[:, i:i + 1], stQ[:, i:i + 1], 1.0 / DM,
                                             stV[:, i:i + 1], ALU.mult, ALU.subtract)
                p.inc(ins, "vst" + tag)
                p.w(v, "rs" + tag, i + 1)
                ins = v.scalar_tensor_tensor(stN[:, i:i + 1], stM[:, i:i + 1], -1.0,
                                             stR[:, i:i + 1], ALU.mult, ALU.mult)
                p.inc(ins, "nmr" + tag)

        @blk.scalar
        def _(s):
            for i in range(NT):
                p.w(s, "xld" + tag, ((i + 1)))
                ins = s.activation(sq, xb[i % 2], AF.Square, accum_out=stQ[:, i:i + 1])
                p.inc(ins, "sq" + tag)
                p.w(s, "vst" + tag, i + 1)
                ins = s.activation(stR[:, i:i + 1], stV[:, i:i + 1], AF.Ln, bias=t_eps, scale=1.0)
                p.bar(s, ins, "zs")
                ins = s.activation(stR[:, i:i + 1], stR[:, i:i + 1], AF.Exp, scale=-0.5)
                p.inc(ins, "rs" + tag)
                p.w(s, "nmr" + tag, i + 1)
                if i >= 2:
                    p.w(s, "hsp" + tag, ((i - 1)))
                ins = s.activation(hst[i % 2], xb[i % 2], AF.Identity,
                                   bias=stN[:, i:i + 1], scale=stR[:, i:i + 1])
                p.inc(ins, "hn" + tag)
                p.w(s, "hn" + tag, i + 1)
                ins = s.dma_start(out=s_h[i * 128:(i + 1) * 128, :], in_=hst[i % 2])
                p.inc(ins, "zds", 16)
                fol(s, "hsp" + tag)

    def p_hT(tag):
        @blk.sync
        def _(sy):
            p.wall(sy, "hsp" + tag)
            for f in range(NC):
                ins = sy.dma_start(out=hT[:, f * T:(f + 1) * T],
                                   in_=s_h[:, f * 128:(f + 1) * 128], transpose=True)
                p.inc(ins, "hT" + tag, 16)

    def gates_head(tag, hd, ktb_h, babs, bl, bA, bksn):
        kap = ktb_h[:, 64:64 + T]

        @blk.scalar
        def _(s):
            p.w(s, "kev" + tag, hd * 4 + 4)
            if hd >= 1:
                p.w(s, "g_fin" + tag, hd)
            ins = s.activation(babs, kap, AF.Abs)
            p.inc(ins, "g_a" + tag)
            ins = s.activation(bksn, babs, AF.Exp, scale=-1.0)
            p.inc(ins, "g_e" + tag)
            ins = s.activation(bl, bksn, AF.Ln, bias=1.0, scale=1.0)
            p.inc(ins, "g_l" + tag)

        @blk.vector
        def _(v):
            p.w(v, "g_l" + tag, hd + 1)
            ins = v.scalar_tensor_tensor(bl, kap, 0.0, bl, ALU.min, ALU.subtract)
            p.inc(ins, "g_g" + tag)
            ins = v.tensor_tensor_scan(bA, bl, bl, 0.0, ALU.add, ALU.bypass)
            p.inc(ins, "g_A" + tag)

        @blk.scalar
        def _(s):
            p.w(s, "g_g" + tag, hd + 1)
            ins = s.activation(babs, kap, AF.Exp)
            p.inc(ins, "g_x" + tag)

        @blk.tensor
        def _(t):
            if hd == 0:
                p.w(t, "ldcg", 6)
            p.w(t, "g_x" + tag, hd + 1)
            for sl in range(4):
                n = hd * 4 + sl
                if n >= 2:
                    p.w(t, "g_r" + tag, n - 1)
                ins = t.matmul(psum[6 + (n % 2)][:, 0:512], t_ones,
                               babs[:, sl * 512:(sl + 1) * 512], start=True, stop=True)
                p.inc(ins, "g_d" + tag)

        @blk.scalar
        def _(s):
            for sl in range(4):
                n = hd * 4 + sl
                p.w(s, "g_d" + tag, n + 1)
                s.activation(bksn[:, sl * 512:(sl + 1) * 512], psum[6 + (n % 2)][:, 0:512], AF.Ln)
                ins = s.activation(bksn[:, sl * 512:(sl + 1) * 512],
                                   bksn[:, sl * 512:(sl + 1) * 512], AF.Exp, scale=-1.0)
                p.inc(ins, "g_r" + tag)

        @blk.vector
        def _(v):
            p.w(v, "g_r" + tag, hd * 4 + 4)
            ins = v.tensor_tensor(bksn, bksn, babs, ALU.mult)
            p.inc(ins, "g_k" + tag)

    def p_kvg(tag, own):
        b0 = B.mark()
        wkb = [B.take(NC * 128, BF16)]
        ktb = [B.take(2112, BF16) for _ in range(2)]
        babs = B.take(T, BF16)
        bl = B.take(T, BF16)
        bA = B.take(T, F32)
        bksn = B.take(T, BF16)
        bw = B.take(2112, BF16)
        ksg = B.take(2112, BF16)
        B.reset(b0)
        bB16 = bl
        bout = bksn

        @blk.gpsimd
        def _(g):
            for hd in range(KVH):
                if hd >= 1:
                    p.w(g, "kmm" + tag, hd * 4)
                elif own:
                    p.w(g, "vmmp", 2 * NT)
                wv_ = d_wk[:, hd * 128:(hd + 1) * 128].rearrange("(c p) j -> p c j", p=128)
                ins = g.dma_start(out=wkb[0].rearrange("p (c j) -> p c j", j=128), in_=wv_)
                p.inc(ins, "wk" + tag, 16)

        for hd in range(KVH):
            @blk.tensor
            def _(t, hd=hd):
                if hd == 0:
                    p.wall(t, "hT" + tag)
                    if own:
                        p.w(t, "Sini", 2)
                p.w(t, "wk" + tag, ((hd + 1)))
                if hd >= 1:
                    p.w(t, "kev" + tag, hd * 4)
                for sl in range(4):
                    pb = psum[sl]
                    for kc in range(NC):
                        ins = t.matmul(pb[:, 0:512], wkb[0][:, kc * 128:(kc + 1) * 128],
                                       hT[:, kc * T + sl * 512: kc * T + sl * 512 + 512],
                                       start=(kc == 0), stop=(kc == NC - 1))
                    p.inc(ins, "kmm" + tag)

            @blk.scalar
            def _(s, hd=hd):
                if hd >= 2:
                    p.w(s, "g_fin" + tag, hd - 1)
                for sl in range(4):
                    p.w(s, "kmm" + tag, hd * 4 + sl + 1)
                    ins = s.copy(ktb[hd % 2][:, 64 + sl * 512: 64 + sl * 512 + 512],
                                 psum[sl][:, 0:512])
                    p.inc(ins, "kev" + tag)

            if own:
                @blk.vector
                def _(v, hd=hd):
                    p.w(v, "kev" + tag, hd * 4 + 4)
                    p.w(v, "khalo", 1)
                    ins = v.tensor_copy(ktb[hd % 2][:, 0:64], t_khalo[:, hd * 64:(hd + 1) * 64])
                    p.bar(v, ins, "zv")
                    ins = v.engine_nop()
                    p.inc(ins, "khc" + tag)
            else:
                @blk.vector
                def _(v, hd=hd):
                    p.w(v, "kev" + tag, hd * 4 + 4)
                    ins = v.tensor_copy(t_khalo[:, hd * 64:(hd + 1) * 64],
                                        ktb[hd % 2][:, 64 + T - 64: 64 + T])
                    p.inc(ins, "khs" + tag)
                    if hd == KVH - 1:
                        ins = v.engine_nop()
                        p.inc(ins, "khalo")

            gates_head(tag, hd, ktb[hd % 2], babs, bl, bA, bksn)

            if not own:
                @blk.vector
                def _(v, hd=hd):
                    p.w(v, "g_A" + tag, hd + 1)
                    if hd >= 1:
                        p.w(v, "pe" + tag, hd)
                    ins = v.tensor_scalar(t_c1[:, 0:1], bA[:, T - 1:T], 1.0 / GN, None, ALU.mult)
                    p.bar(v, ins, "zv")
                    ins = v.engine_nop()
                    p.inc(ins, "pb" + tag)

                @blk.scalar
                def _(s, hd=hd):
                    p.w(s, "pb" + tag, hd + 1)
                    ins = s.activation(babs, bA, AF.Exp, bias=t_c1[:, 0:1], scale=-1.0 / GN)
                    p.inc(ins, "pe" + tag)

                @blk.vector
                def _(v, hd=hd):
                    p.w(v, "pe" + tag, hd + 1)
                    p.w(v, "g_k" + tag, hd + 1)
                    ins = v.tensor_tensor(bksn, bksn, babs, ALU.mult)
                    p.bar(v, ins, "zv")
                    ins = v.engine_nop()
                    p.inc(ins, "kh" + tag)

                @blk.scalar
                def _(s, hd=hd):
                    p.w(s, "kh" + tag, hd + 1)
                    ins = s.dma_start(out=s_kh[:, hd * T:(hd + 1) * T], in_=bksn)
                    p.inc(ins, "khsp" + tag, 16)
                    p.w(s, "khsp" + tag, ((hd + 1)))
                    ins = s.nop()
                    p.inc(ins, "g_fin" + tag)
            else:
                @blk.vector
                def _(v, hd=hd):
                    p.w(v, "g_A" + tag, hd + 1)
                    if hd >= 1:
                        p.w(v, "seNB" + tag, hd)
                    v.memset(t_R[:, 0:1], 0.0)
                    ins = v.tensor_scalar(
                        t_R[:, 1:16].rearrange("p (n o) -> p n o", o=1),
                        bA.rearrange("p (n c) -> p n c", c=128)[:, 0:15, 127:128],
                        1.0 / GN, None, ALU.mult)
                    p.inc(ins, "rR" + tag)
                    v.memset(t_c1[:, 0:1], 0.0)
                    p.w(v, "rR" + tag, hd + 1)
                    ins = v.tensor_sub(t_c1[:, 1:16], t_R[:, 1:16], t_R[:, 0:15])
                    p.bar(v, ins, "zv")
                    ins = v.engine_nop()
                    p.inc(ins, "rD" + tag)
                    ins = v.scalar_tensor_tensor(
                        bB16.rearrange("p (n c) -> p n c", c=128), bA.rearrange("p (n c) -> p n c", c=128),
                        1.0 / GN,
                        t_R.rearrange("p (a o) -> p a o", o=1).broadcast_to([128, 16, 128]),
                        ALU.mult, ALU.subtract)
                    p.bar(v, ins, "zv")
                    ins = v.engine_nop()
                    p.inc(ins, "rB" + tag)

                @blk.scalar
                def _(s, hd=hd):
                    p.w(s, "rD" + tag, hd + 1)
                    ins = s.activation(t_D[:, hd * 16:hd * 16 + 16], t_c1[:, 0:16], AF.Exp)
                    p.inc(ins, "sD" + tag)
                    p.w(s, "rB" + tag, hd + 1)
                    if hd >= 1:
                        p.w(s, "spw34" + tag, (hd))
                    ins = s.activation(bw[:, 0:T], bB16, AF.Exp)
                    p.inc(ins, "seB" + tag)
                    p.w(s, "seB" + tag, hd + 1)
                    ins = s.dma_start(out=s_eB[:, hd * T:(hd + 1) * T],
                                      in_=bw[:, 0:T])
                    p.inc(ins, "speB" + tag, 16)
                    p.w(s, "g_k" + tag, hd + 1)
                    ins = s.activation(babs, bB16, AF.Exp, scale=-1.0)
                    p.inc(ins, "seNB" + tag)

                @blk.vector
                def _(v, hd=hd):
                    p.w(v, "seNB" + tag, hd + 1)
                    ins = v.tensor_tensor(bksn, bksn, babs, ALU.mult)
                    p.bar(v, ins, "zv")
                    ins = v.engine_nop()
                    p.inc(ins, "vkt" + tag)

                @blk.scalar
                def _(s, hd=hd):
                    p.w(s, "vkt" + tag, hd + 1)
                    ins = s.dma_start(out=s_kt[:, hd * T:(hd + 1) * T], in_=bksn)
                    p.inc(ins, "spkt" + tag, 16)

                SL = [(0, 512), (512, 512), (1024, 512), (1536, 512), (2048, 64)]
                for r in range(2):
                    @blk.tensor
                    def _(t, hd=hd, r=r):
                        if hd == 0 and r == 0:
                            p.w(t, "ldcg", 6)
                        p.w(t, "khc" + tag, hd + 1)
                        if hd > 0 or r > 0:
                            p.w(t, "ksu" + tag, 2 * hd + r)
                        for j, (off, sz) in enumerate(SL):
                            n = (2 * hd + r) * 5 + j
                            if n >= 2:
                                p.w(t, "kse" + tag, n - 1)
                            ins = t.matmul(psum[4 + (n % 2)][:, 0:sz], t_psig,
                                           ktb[hd % 2][:, off:off + sz],
                                           start=True, stop=True)
                            p.inc(ins, "ksg" + tag)

                    @blk.scalar
                    def _(s, hd=hd, r=r):
                        for j, (off, sz) in enumerate(SL):
                            n = (2 * hd + r) * 5 + j
                            p.w(s, "ksg" + tag, n + 1)
                            if sz < 512:
                                s.activation(t_zz[:, 1:2], t_zz[:, 0:1], AF.Abs)
                            ins = s.copy(ksg[:, off:off + sz], psum[4 + (n % 2)][:, 0:sz])
                            p.inc(ins, "kse" + tag)

                @blk.vector
                def _(v, hd=hd):
                    p.w(v, "kse" + tag, (2 * hd + 1) * 5)
                    p.w(v, "speB" + tag, ((hd + 1)))
                    kt = ktb[hd % 2]
                    v.tensor_mul(bw, t_T1, kt)
                    v.tensor_tensor(ksg, t_T2, ksg, ALU.mult)
                    ins = v.tensor_add(bw, bw, ksg)
                    p.bar(v, ins, "zv")
                    ins = v.engine_nop()
                    p.inc(ins, "w12" + tag)
                    ins = v.engine_nop()
                    p.inc(ins, "ksu" + tag)

                @blk.scalar
                def _(s, hd=hd):
                    p.w(s, "w12" + tag, hd + 1)
                    ins = s.dma_start(out=s_w12[:, hd * 2112:(hd + 1) * 2112], in_=bw)
                    p.inc(ins, "spw12" + tag, 16)

                @blk.vector
                def _(v, hd=hd):
                    p.w(v, "kse" + tag, (2 * hd + 2) * 5)
                    p.w(v, "spw12" + tag, ((hd + 1)))
                    kt = ktb[hd % 2]
                    v.tensor_tensor(ksg, t_T1s, ksg, ALU.mult)
                    v.tensor_mul(bw, t_T2s, kt)
                    ins = v.tensor_add(bw, bw, ksg)
                    p.bar(v, ins, "zv")
                    ins = v.engine_nop()
                    p.inc(ins, "w34" + tag)
                    ins = v.engine_nop()
                    p.inc(ins, "ksu" + tag)
                    ins = v.engine_nop()
                    p.inc(ins, "g_fin" + tag)

                @blk.scalar
                def _(s, hd=hd):
                    p.w(s, "w34" + tag, hd + 1)
                    ins = s.dma_start(out=s_w34[:, hd * 2112:(hd + 1) * 2112], in_=bw)
                    p.inc(ins, "spw34" + tag, 16)

        B.reset(b0)
        wvb = B.take(NC * 512, BF16)
        vstg = [B.take(512, BF16) for _ in range(2)]
        B.reset(b0)

        @blk.gpsimd
        def _(g):
            p.w(g, "g_fin" + tag, KVH)
            if own:
                p.w(g, "spkt" + tag, KVH)
            for dvs in range(2):
                if dvs == 1:
                    p.w(g, "vmm" + tag, NT)
                wv_ = d_wv[:, dvs * 512:(dvs + 1) * 512].rearrange("(c p) j -> p c j", p=128)
                ins = g.dma_start(out=wvb.rearrange("p (c j) -> p c j", j=512), in_=wv_)
                p.inc(ins, "wv" + tag, 16)

        vspm = {}

        @blk.tensor
        def _(t):
            p.w(t, "kev" + tag, KVH * 4)
            p.w(t, "g_d" + tag, KVH * 4)
            for dvs in range(2):
                p.w(t, "wv" + tag, ((dvs + 1)))
                for tt in range(NT):
                    n = dvs * NT + tt
                    bk = psum[n % 4]
                    if n >= 4:
                        p.w(t, "vevc" + tag, n - 3)
                    for kc in range(NC):
                        ins = t.matmul(bk[:, 0:512],
                                       hT[:, kc * T + tt * 128: kc * T + tt * 128 + 128],
                                       wvb[:, kc * 512:(kc + 1) * 512],
                                       start=(kc == 0), stop=(kc == NC - 1))
                    p.inc(ins, "vmm" + tag)

        @blk.scalar
        def _(s):
            for dvs in range(2):
                for tt in range(NT):
                    n = dvs * NT + tt
                    p.w(s, "vmm" + tag, n + 1)
                    if n >= 2:
                        p.w(s, "vsp" + tag, vspm[n - 2])
                    ins = s.copy(vstg[n % 2], psum[n % 4][:, 0:512])
                    p.inc(ins, "vevc" + tag)
                    p.w(s, "vevc" + tag, n + 1)
                    ins = s.dma_start(out=s_V[1 + tt, :, dvs * 512:dvs * 512 + 512],
                                      in_=vstg[n % 2])
                    p.inc(ins, "vsp" + tag, 16)
                    if (not own) and tt == NT - 1:
                        ins = s.dma_start(out=s_V[0, :, dvs * 512:dvs * 512 + 512],
                                          in_=vstg[n % 2])
                        p.inc(ins, "vsp" + tag, 16)
                        if dvs == 1:
                            fol(s, "vspf")
                    vspm[n] = p.n("vsp" + tag)

    def p2_state():
        b0 = B.mark()
        B.reset(58368)
        bkh = B.take(T, BF16)
        bvh = B.take(1024, BF16)
        bkn = [B.take(128, BF16) for _ in range(2)]
        B.reset(b0)

        @blk.sync
        def _(sy):
            p.w(sy, "khspp", KVH)
            p.wall(sy, "vspp")
            p.w(sy, "vspf", 1)
            for hd in range(KVH):
                if hd >= 1:
                    p.w(sy, "sktr", hd * NT)
                ins = sy.dma_start(out=bkh, in_=s_kh[:, hd * T:(hd + 1) * T])
                p.inc(ins, "zd", 16)
                fol(sy, "skh")
                for g2 in range(2):
                    n8 = hd * 2 + g2
                    if n8 >= 1:
                        p.w(sy, "smm", n8 * 8)
                    vv = s_V[1 + g2 * 8:1 + g2 * 8 + 8, :, hd * 128:(hd + 1) * 128]
                    ins = sy.dma_start(out=bvh.rearrange("p (n c) -> p n c", c=128),
                                       in_=vv.rearrange("n p c -> p n c"))
                    p.inc(ins, "zd", 16)
                    fol(sy, "skv")

        @blk.tensor
        def _(t):
            p.w(t, "ldc", 5)
            for hd in range(KVH):
                p.w(t, "skh", hd + 1)
                for tt in range(NT):
                    n = hd * NT + tt
                    g2 = tt // 8
                    if n >= 2:
                        p.w(t, "skne", n - 1)
                    ins = t.transpose(psb[4 + (n % 2)][:, 0:128],
                                      bkh[:, tt * 128:(tt + 1) * 128], t_id)
                    p.inc(ins, "sktr")
                    p.w(t, "skne", n + 1)
                    p.w(t, "skv", hd * 2 + g2 + 1)
                    ins = t.matmul(psum[2 + (hd // 4)][:, (hd % 4) * 128:(hd % 4) * 128 + 128],
                                   bkn[n % 2],
                                   bvh[:, (tt % 8) * 128:(tt % 8) * 128 + 128],
                                   start=(hd % 4 == 0 and tt == 0),
                                   stop=(hd % 4 == 3 and tt == NT - 1))
                    p.inc(ins, "smm")

        @blk.scalar
        def _(s):
            for hd in range(KVH):
                for tt in range(NT):
                    n = hd * NT + tt
                    p.w(s, "sktr", n + 1)
                    if n >= 2:
                        p.w(s, "smm", n - 1)
                    ins = s.copy(bkn[n % 2], psb[4 + (n % 2)][:, 0:128])
                    p.bar(s, ins, "zs")
                    ins = s.nop()
                    p.inc(ins, "skne")

        @blk.vector
        def _(v):
            p.w(v, "ldc", 5)
            p.w(v, "smm", KVH * NT)
            v.tensor_copy(t_zz[:, 1:2], t_zz[:, 0:1])
            v.tensor_copy(t_zz[:, 1:2], t_zz[:, 0:1])
            for half in range(2):
                ins = v.tensor_scalar(t_S[:, half * 512:(half + 1) * 512],
                                      psum[2 + half][:, 0:512], t_fl2[:, 0:1], None, ALU.mult)
                p.inc(ins, "Sini")

    def p3_q():
        b0 = B.mark()
        wqb = [B.take(NC * 128, BF16) for _ in range(2)]
        qst = [B.take(T, BF16) for _ in range(2)]
        bE = B.take(T, BF16)
        bRc = B.take(T, BF16)
        bq = B.take(T, BF16)
        beB = B.take(T, BF16)
        B.reset(b0)

        @blk.gpsimd
        def _(g):
            for hq in range(H):
                if hq >= 2:
                    p.w(g, "qmm", (hq - 1) * 4)
                else:
                    p.w(g, "vmmo", 2 * NT)
                wv_ = d_wq[:, hq * 128:(hq + 1) * 128].rearrange("(c p) j -> p c j", p=128)
                ins = g.dma_start(out=wqb[hq % 2].rearrange("p (c j) -> p c j", j=128), in_=wv_)
                p.inc(ins, "wq", 16)

        @blk.sync
        def _(sy):
            p.w(sy, "speBo", (KVH))
            p.wall(sy, "vspo")
            for hd in range(KVH):
                if hd >= 1:
                    p.w(sy, "qqt", 4 * hd)
                ins = sy.dma_start(out=beB, in_=s_eB[:, hd * T:(hd + 1) * T])
                p.inc(ins, "zd", 16)
                fol(sy, "ldeB")

        for hq in range(H):
            @blk.tensor
            def _(t, hq=hq):
                if hq == 0:
                    p.w(t, "vevco", 2 * NT)
                p.w(t, "wq", ((hq + 1)))
                if hq >= 1:
                    p.w(t, "qev", hq * 4)
                for sl in range(4):
                    pb = psum[sl]
                    for kc in range(NC):
                        ins = t.matmul(pb[:, 0:512], wqb[hq % 2][:, kc * 128:(kc + 1) * 128],
                                       hT[:, kc * T + sl * 512: kc * T + sl * 512 + 512],
                                       start=(kc == 0), stop=(kc == NC - 1))
                    p.inc(ins, "qmm")

            @blk.scalar
            def _(s, hq=hq):
                if hq >= 2:
                    p.w(s, "qspl", (3 * (hq - 1)))
                if hq >= 1:
                    p.w(s, "qqt", hq)
                for sl in range(4):
                    p.w(s, "qmm", hq * 4 + sl + 1)
                    ins = s.copy(qst[hq % 2][:, sl * 512:(sl + 1) * 512],
                                 psum[sl][:, 0:512])
                    p.inc(ins, "qev")
                ins = s.activation(bE, qst[hq % 2], AF.Exp)
                p.inc(ins, "qE")

            @blk.tensor
            def _(t, hq=hq):
                p.w(t, "qE", hq + 1)
                for sl in range(4):
                    n = hq * 4 + sl
                    if n >= 2:
                        p.w(t, "qR", n - 1)
                    ins = t.matmul(psum[6 + (n % 2)][:, 0:512], t_ones,
                                   bE[:, sl * 512:(sl + 1) * 512], start=True, stop=True)
                    p.inc(ins, "qD")

            @blk.scalar
            def _(s, hq=hq):
                for sl in range(4):
                    n = hq * 4 + sl
                    p.w(s, "qD", n + 1)
                    s.activation(bRc[:, sl * 512:(sl + 1) * 512], psum[6 + (n % 2)][:, 0:512], AF.Ln)
                    ins = s.activation(bRc[:, sl * 512:(sl + 1) * 512],
                                       bRc[:, sl * 512:(sl + 1) * 512], AF.Exp, scale=-1.0)
                    p.inc(ins, "qR")

            @blk.vector
            def _(v, hq=hq):
                p.w(v, "qR", hq * 4 + 4)
                p.w(v, "ldeB", ((hq // 4 + 1)))
                if hq >= 1:
                    p.w(v, "qspl", ((3 * hq)))
                v.tensor_mul(bq, bE, bRc)
                ins = v.tensor_tensor(bq, bq, beB, ALU.mult)
                p.bar(v, ins, "zv")
                ins = v.engine_nop()
                p.inc(ins, "qqt")

            @blk.scalar
            def _(s, hq=hq):
                p.w(s, "qqt", hq + 1)
                ins = s.dma_start(out=s_qt[hq], in_=bq)
                p.inc(ins, "qspl", 16)

            @blk.vector
            def _(v, hq=hq):
                if hq == 0:
                    p.w(v, "ldcg", 6)
                p.w(v, "qspl", ((3 * hq + 1)))
                ins = v.tensor_mul(bq, t_T1[:, 64:64 + T], qst[hq % 2])
                p.bar(v, ins, "zv")
                ins = v.engine_nop()
                p.inc(ins, "qu1")

            @blk.scalar
            def _(s, hq=hq):
                p.w(s, "qu1", hq + 1)
                ins = s.dma_start(out=s_u1[hq], in_=bq)
                p.inc(ins, "qspl", 16)

            @blk.vector
            def _(v, hq=hq):
                p.w(v, "qspl", ((3 * hq + 2)))
                ins = v.tensor_mul(bq, t_T2s[:, 64:64 + T], qst[hq % 2])
                p.bar(v, ins, "zv")
                ins = v.engine_nop()
                p.inc(ins, "qu2")

            @blk.scalar
            def _(s, hq=hq):
                p.w(s, "qu2", hq + 1)
                ins = s.dma_start(out=s_u2[hq], in_=bq)
                p.inc(ins, "qspl", 16)
                if hq == H - 1:
                    fol(s, "qspf")

    def p7_dequant(apos):
        a0 = A.mark()
        A.reset(apos)
        bpk = [A.take(512, I32) for _ in range(2)]
        bsc = [A.take(32, F32) for _ in range(2)]
        bni = A.take(512, I32)
        bnb = A.take(512, BF16)
        bst = A.take(DM, BF16)
        A.reset(a0)

        @blk.sync
        def _(sy):
            p.wall(sy, "ospw1")
            for ot in range(32):
                if ot >= 2:
                    p.w(sy, "dqv", (ot - 1) * 8)
                ins = sy.dma_start(out=bpk[ot % 2], in_=d_pk[ot * 128:(ot + 1) * 128, :])
                p.inc(ins, "zd", 16)
                ins = sy.dma_start(out=bsc[ot % 2], in_=d_sc[ot * 128:(ot + 1) * 128, :])
                p.inc(ins, "zd", 16)
                fol(sy, "dql")

        @blk.vector
        def _(v):
            for ot in range(32):
                p.w(v, "dql", ot + 1)
                if ot >= 1:
                    p.w(v, "dqs", (ot))
                sc3 = bsc[ot % 2].rearrange("p (g o) -> p g o", o=1).broadcast_to([128, 32, 16])
                for e in range(8):
                    v.tensor_scalar(bni, bpk[ot % 2], 4 * e, 28,
                                    ALU.arith_shift_left, ALU.arith_shift_right)
                    v.tensor_copy(bnb, bni)
                    ins = v.tensor_tensor(
                        bst.rearrange("p (e i) -> p e i", e=8)[:, e],
                        bnb.rearrange("p (g o) -> p g o", o=16), sc3, ALU.mult)
                    if e == 7:
                        p.bar(v, ins, "zv")
                        ins = v.engine_nop()
                    p.inc(ins, "dqv")

        @blk.scalar
        def _(s):
            for ot in range(32):
                p.w(s, "dqv", (ot + 1) * 8)
                ins = s.dma_start(out=s_wdq[ot * 128:(ot + 1) * 128, :], in_=bst)
                p.inc(ins, "dqs", 16)

    def p4_attn(hh, attn, apos):
        a0 = A.mark()
        A.reset(apos)
        qs = [A.take(H * 128, BF16) for _ in range(2)]
        us1 = [A.take(H * 128, BF16) for _ in range(2)]
        us2 = [A.take(H * 128, BF16) for _ in range(2)]
        A.reset(a0)
        b0 = B.mark()
        B.reset(0)
        kts = [B.take(KVH * 128, BF16) for _ in range(2)]
        w12s = [B.take(KVH * 192, BF16) for _ in range(2)]
        w34s = [B.take(KVH * 192, BF16) for _ in range(2)]
        vri = [B.take(1024, BF16) for _ in range(3)]
        bkn = B.take(KVH * 128, BF16)
        bPT = [B.take(512, BF16) for _ in range(2)]
        bsw = B.take(192, F32)
        bswE = B.take(192, BF16)
        bswP = B.take(192, BF16)
        bPta = B.take(128, BF16)
        bPtb = B.take(128, BF16)
        brc = B.take(4, F32)
        B.reset(b0)

        for c in range(8):
            gc = hh * 8 + c
            par = gc % 2
            cs = gc * CH

            @blk.sync
            def _(sy, gc=gc, par=par, cs=cs):
                if gc == 0:
                    p.w(sy, "ldc", 5)
                    p.w(sy, "Sini", 2)
                    p.w(sy, "spkto", (KVH))
                    p.w(sy, "spw12o", (KVH))
                    p.w(sy, "spw34o", (KVH))
                    p.w(sy, "qspl", (3 * H))
                    p.w(sy, "qspf", 1)
                    p.wall(sy, "vspo")
                    p.wall(sy, "vspp")
                if gc == 8:
                    p.w(sy, "ommw0", 64)
                if c == 0:
                    for j in (gc, gc + 1):
                        ins = sy.dma_start(out=vri[(j + 1) % 3], in_=s_V[j])
                        p.inc(ins, "zd", 16)
                    fol(sy, "vrl")
                if gc >= 2:
                    p.w(sy, "p4o", (gc - 1) * KVH)
                ins = sy.dma_start(out=qs[par].rearrange("p (h t) -> p h t", t=CH),
                                   in_=s_qt[:, :, cs:cs + CH].rearrange("h p t -> p h t"))
                p.inc(ins, "zd", 16)
                ins = sy.dma_start(out=us1[par].rearrange("p (h t) -> p h t", t=CH),
                                   in_=s_u1[:, :, cs:cs + CH].rearrange("h p t -> p h t"))
                p.inc(ins, "zd", 16)
                ins = sy.dma_start(out=us2[par].rearrange("p (h t) -> p h t", t=CH),
                                   in_=s_u2[:, :, cs:cs + CH].rearrange("h p t -> p h t"))
                p.inc(ins, "zd", 16)
                fol(sy, "p4q")
                ins = sy.dma_start(out=kts[par].rearrange("p (h t) -> p h t", t=CH),
                                   in_=s_kt.rearrange("p (h t) -> p h t", t=T)[:, :, cs:cs + CH])
                p.inc(ins, "zd", 16)
                ins = sy.dma_start(out=w12s[par].rearrange("p (h t) -> p h t", t=192),
                                   in_=s_w12.rearrange("p (h t) -> p h t", t=2112)[:, :, cs:cs + 192])
                p.inc(ins, "zd", 16)
                ins = sy.dma_start(out=w34s[par].rearrange("p (h t) -> p h t", t=192),
                                   in_=s_w34.rearrange("p (h t) -> p h t", t=2112)[:, :, cs:cs + 192])
                p.inc(ins, "zd", 16)
                fol(sy, "p4k")
                if gc + 2 <= NT:
                    if gc >= 1:
                        p.w(sy, "p4o", gc * KVH)
                    ins = sy.dma_start(out=vri[(gc + 3) % 3], in_=s_V[gc + 2])
                    p.inc(ins, "zd", 16)
                    fol(sy, "vrl")

            qv = p.n("p4q")
            kv_ = p.n("p4k")
            vv_ = p.n("vrl")

            @blk.vector
            def _(v, gc=gc):
                p.w(v, "p4ds", gc * 2)
                if gc >= 1:
                    p.w(v, "p4oi", gc * KVH)
                else:
                    p.w(v, "Sini", 2)
                    p.w(v, "sDo", KVH)
                dap = t_D.rearrange("p (h n) -> p h n", n=NCH)[:, :, gc:gc + 1] \
                    .broadcast_to([128, KVH, 128])
                ins = v.tensor_tensor(t_S.rearrange("p (h d) -> p h d", d=128),
                                      t_S.rearrange("p (h d) -> p h d", d=128), dap, ALU.mult)
                p.inc(ins, "p4rs")
                ins = v.tensor_copy(t_Sb, t_S)
                p.bar(v, ins, "zv")
                ins = v.engine_nop()
                p.inc(ins, "p4sb")

            for hd in range(KVH):
                @blk.tensor
                def _(t, gc=gc, par=par, hd=hd, qv=qv, kv_=kv_, vv_=vv_):
                    if hd == 0:
                        p.w(t, "p4q", qv)
                        p.w(t, "p4k", kv_)
                        p.w(t, "vrl", vv_)
                    n = gc * KVH + hd
                    if n >= 1:
                        p.w(t, "p4pt", n)
                    ins = t.matmul(psum[0][:, 0:512], kts[par][:, hd * 128:(hd + 1) * 128],
                                   qs[par][:, hd * 512:(hd + 1) * 512], start=True, stop=True)
                    p.inc(ins, "p4mpt")
                    if n >= 1:
                        p.w(t, "p4kne", n)
                    ins = t.transpose(psb[5][:, 0:128],
                                      kts[par][:, hd * 128:(hd + 1) * 128], t_id)
                    p.inc(ins, "p4ktr")

                @blk.vector
                def _(v, gc=gc, hd=hd):
                    n = gc * KVH + hd
                    p.w(v, "p4mpt", n + 1)
                    t4 = t_tril.rearrange("p (o t) -> p o t", o=1).broadcast_to([128, 4, 128])
                    ins = v.tensor_tensor(bPT[hd % 2].rearrange("p (g t) -> p g t", t=128),
                                          psum[0].ap().rearrange("p (g t) -> p g t", t=128),
                                          t4, ALU.mult)
                    p.bar(v, ins, "zv")
                    ins = v.engine_nop()
                    p.inc(ins, "p4pt")

                @blk.scalar
                def _(s, gc=gc, hd=hd):
                    n = gc * KVH + hd
                    p.w(s, "p4ktr", n + 1)
                    if gc >= 1:
                        p.w(s, "p4dsm", gc * 2)
                    ins = s.copy(bkn[:, hd * 128:(hd + 1) * 128],
                                 psb[5][:, 0:128])
                    p.bar(s, ins, "zs")
                    ins = s.nop()
                    p.inc(ins, "p4kne")

                ob = psum[1 + (hd % 2)]

                @blk.tensor
                def _(t, gc=gc, par=par, hd=hd, ob=ob):
                    n = gc * KVH + hd
                    p.w(t, "p4sb", gc + 1)
                    p.w(t, "p4pt", n + 1)
                    if n >= 2:
                        p.w(t, "p4o", n - 1)
                    t.matmul(ob[:, 0:512], t_Sb[:, hd * 128:(hd + 1) * 128],
                             qs[par][:, hd * 512:(hd + 1) * 512], start=True, stop=False)
                    vcur = vri[(gc + 2) % 3]
                    ins = t.matmul(ob[:, 0:512], vcur[:, hd * 128:(hd + 1) * 128],
                                   bPT[hd % 2], start=False, stop=False)
                    p.inc(ins, "p4oi")

                for qh in range(4):
                    hq = hd * 4 + qh
                    scb = psum[3]

                    @blk.tensor
                    def _(t, gc=gc, par=par, hd=hd, qh=qh, hq=hq, scb=scb):
                        n = (gc * KVH + hd) * 4 + qh
                        if n >= 1:
                            p.w(t, "p4swm", n)
                        t.matmul(scb[:, 0:192], us1[par][:, hq * 128:(hq + 1) * 128],
                                 w12s[par][:, hd * 192:(hd + 1) * 192], start=True, stop=False)
                        ins = t.matmul(scb[:, 0:192], us2[par][:, hq * 128:(hq + 1) * 128],
                                       w34s[par][:, hd * 192:(hd + 1) * 192], start=False, stop=True)
                        p.inc(ins, "p4sc")

                    @blk.vector
                    def _(v, gc=gc, hd=hd, qh=qh, scb=scb):
                        n = (gc * KVH + hd) * 4 + qh
                        p.w(v, "p4sc", n + 1)
                        p.w(v, "p4swe", n)
                        msk = t_m0 if gc == 0 else t_mi
                        ins = v.tensor_add(bsw, scb[:, 0:192], msk)
                        p.bar(v, ins, "zv")
                        ins = v.engine_nop()
                        p.inc(ins, "p4swm")

                    @blk.scalar
                    def _(s, gc=gc, hd=hd, qh=qh):
                        n = (gc * KVH + hd) * 4 + qh
                        p.w(s, "p4swm", n + 1)
                        p.w(s, "p4swp", n)
                        ins = s.activation(bswE, bsw, AF.Exp, scale=ISQD,
                                           accum_out=brc[:, 0:1])
                        p.bar(s, ins, "zs")
                        ins = s.nop()
                        p.inc(ins, "p4swe")

                    @blk.vector
                    def _(v, gc=gc, hd=hd, qh=qh):
                        n = (gc * KVH + hd) * 4 + qh
                        p.w(v, "p4swe", n + 1)
                        p.w(v, "p4pte", (n))
                        ins = v.reciprocal(brc[:, 1:2], brc[:, 0:1])
                        p.bar(v, ins, "zv")
                        ins = v.tensor_scalar(bswP, bswE, brc[:, 1:2], None, ALU.mult)
                        p.bar(v, ins, "zv")
                        ins = v.engine_nop()
                        p.inc(ins, "p4swp")

                    @blk.tensor
                    def _(t, gc=gc, hd=hd, qh=qh):
                        n = (gc * KVH + hd) * 4 + qh
                        p.w(t, "p4swp", n + 1)
                        if n >= 1:
                            p.w(t, "p4pte", (n))
                        t.transpose(psb[6][64:128, 0:128], bswP[:, 0:64], t_id)
                        ins = t.transpose(psb[4][:, 512:640], bswP[:, 64:192], t_id)
                        p.inc(ins, "p4ptr")

                    @blk.scalar
                    def _(s, gc=gc, hd=hd, qh=qh):
                        n = (gc * KVH + hd) * 4 + qh
                        p.w(s, "p4ptr", n + 1)
                        s.copy(bPta[64:128, :], psb[6][64:128, 0:128])
                        ins = s.copy(bPtb, psb[4][:, 512:640])
                        p.bar(s, ins, "zs")
                        ins = s.nop()
                        p.inc(ins, "p4pte", 2)

                    @blk.tensor
                    def _(t, gc=gc, hd=hd, qh=qh, ob=ob):
                        n = (gc * KVH + hd) * 4 + qh
                        p.w(t, "p4pte", ((n + 1)))
                        vprev = vri[(gc + 1) % 3]
                        vcur = vri[(gc + 2) % 3]
                        oc = ob[:, qh * 128:(qh + 1) * 128]
                        t.matmul(oc, vprev[64:128, hd * 128:(hd + 1) * 128],
                                 bPta[64:128, :], start=False, stop=False)
                        last = (qh == 3)
                        ins = t.matmul(oc, vcur[:, hd * 128:(hd + 1) * 128],
                                       bPtb, start=False, stop=last)
                        if last:
                            p.inc(ins, "p4om")

                @blk.scalar
                def _(s, gc=gc, c=c, hd=hd, ob=ob):
                    if gc == 8 and hd == 0:
                        p.w(s, "ommw0", 64)
                        if "dbg_attn" in d_dbg:
                            p.w(s, "dbgw2", 1)
                    p.w(s, "p4om", gc * KVH + hd + 1)
                    s.activation(t_zz[:, 1:2], t_zz[:, 0:1], AF.Abs)
                    dst = attn.rearrange("p (h t) -> p h t", t=1024)[:, 4 * hd:4 * hd + 4,
                                                                    c * 128:(c + 1) * 128]
                    ins = s.mul(dst, ob.ap().rearrange("p (h t) -> p h t", t=128), 0.5)
                    p.inc(ins, "p4o")

                @blk.tensor
                def _(t, gc=gc, hd=hd):
                    p.w(t, "p4kne", gc * KVH + hd + 1)
                    if hd % 4 == 0:
                        p.w(t, "p4ds", gc * 2 + (hd // 4))
                    vcur = vri[(gc + 2) % 3]
                    ins = t.matmul(psum[7][:, (hd % 4) * 128:(hd % 4) * 128 + 128],
                                   bkn[:, hd * 128:(hd + 1) * 128],
                                   vcur[:, hd * 128:(hd + 1) * 128],
                                   start=(hd % 4 == 0), stop=(hd % 4 == 3))
                    if hd % 4 == 3:
                        p.inc(ins, "p4dsm")

                if hd % 4 == 3:
                    @blk.vector
                    def _(v, gc=gc, half=hd // 4):
                        p.w(v, "p4rs", gc + 1)
                        p.w(v, "p4dsm", gc * 2 + half + 1)
                        v.tensor_copy(t_zz[:, 1:2], t_zz[:, 0:1])
                        ins = v.tensor_add(t_S[:, half * 512:(half + 1) * 512],
                                           t_S[:, half * 512:(half + 1) * 512],
                                           psum[7][:, 0:512])
                        p.inc(ins, "p4ds")

        B.reset(b0)


    def p5_wo(hh, attn):
        b0 = B.mark()
        B.reset(0)
        wob = [B.take(NC * 512, BF16) for _ in range(2)]
        B.reset(b0)
        a0 = A.mark()
        A.reset(114688)
        opst = [A.take(512, BF16) for _ in range(2)]
        sqs = A.take(512, BF16)
        A.reset(a0)
        tag = f"w{hh}"
        ospm = {}

        @blk.gpsimd
        def _(g):
            p.w(g, "p4o", (hh + 1) * 64)
            p.w(g, "p4dsm", (hh + 1) * 16)
            for ocb in range(8):
                if ocb >= 2:
                    p.w(g, "omm" + tag, (ocb - 1) * 8)
                wv_ = d_wo[:, ocb * 512:(ocb + 1) * 512].rearrange("(c p) j -> p c j", p=128)
                ins = g.dma_start(out=wob[ocb % 2].rearrange("p (c j) -> p c j", j=512), in_=wv_)
                p.inc(ins, "wo" + tag, 16)

        @blk.tensor
        def _(t):
            p.wall(t, "p4o")
            for ocb in range(8):
                p.w(t, "wo" + tag, ((ocb + 1)))
                for tt in range(8):
                    n = ocb * 8 + tt
                    bk = psum[n % 4]
                    if n >= 4:
                        p.w(t, "oevc" + tag, n - 3)
                    for hc in range(NC):
                        ins = t.matmul(bk[:, 0:512],
                                       attn[:, hc * 1024 + tt * 128: hc * 1024 + tt * 128 + 128],
                                       wob[ocb % 2][:, hc * 512:(hc + 1) * 512],
                                       start=(hc == 0), stop=(hc == NC - 1))
                    p.inc(ins, "omm" + tag)

        @blk.vector
        def _(v):
            for ocb in range(8):
                for tt in range(8):
                    n = ocb * 8 + tt
                    p.w(v, "omm" + tag, n + 1)
                    col = 96 + (hh * 8 + tt) * 8 + ocb
                    ins = v.reduce_sum(t_st[:, col:col + 1], psum[n % 4][:, 0:512], AX.X)
                    p.inc(ins, "ost" + tag)

        @blk.scalar
        def _(s):
            for ocb in range(8):
                for tt in range(8):
                    n = ocb * 8 + tt
                    p.w(s, "ost" + tag, n + 1)
                    if n >= 2:
                        p.w(s, "osp" + tag, ospm[n - 2])
                    col = 224 + (hh * 8 + tt) * 8 + ocb
                    s.activation(sqs, psum[n % 4][:, 0:512], AF.Square,
                                 accum_out=t_st[:, col:col + 1])
                    ins = s.copy(opst[n % 2], psum[n % 4][:, 0:512])
                    p.inc(ins, "oevc" + tag)
                    p.w(s, "oevc" + tag, n + 1)
                    ins = s.dma_start(
                        out=s_op[hh * 1024 + tt * 128: hh * 1024 + tt * 128 + 128,
                                 ocb * 512:(ocb + 1) * 512],
                        in_=opst[n % 2])
                    p.inc(ins, "osp" + tag, 16)
                    ospm[n] = p.n("osp" + tag)

    def p6_x2():
        a0 = A.mark()
        A.reset(0)
        xb = [A.take(DM, F32) for _ in range(2)]
        ob = [A.take(DM, BF16) for _ in range(2)]
        tmp = A.take(DM, F32)
        x2b = [A.take(DM, F32) for _ in range(2)]
        nst = [A.take(DM, BF16) for _ in range(2)]
        pst = [A.take(DM, BF16) for _ in range(2)]
        sqb = tmp
        A.reset(a0)
        stS, stQ = t_st[:, 0:16], t_st[:, 16:32]
        stM, stR = t_st[:, 32:48], t_st[:, 48:64]
        stN, stV = t_st[:, 64:80], t_st[:, 80:96]
        spm = {}

        @blk.sync
        def _(sy):
            p.wall(sy, "ospw0")
            p.wall(sy, "ospw1")
            if "dbg_attn2" in d_dbg:
                p.w(sy, "dbgw3", 1)
            for i in range(NT):
                if i >= 2:
                    p.w(sy, "x6nn", i - 1)
                ins = sy.dma_start(out=xb[i % 2], in_=d_x[i * 128:(i + 1) * 128, :])
                p.inc(ins, "zd", 16)
                ins = sy.dma_start(out=ob[i % 2], in_=s_op[i * 128:(i + 1) * 128, :])
                p.inc(ins, "zd", 16)
                fol(sy, "x6l")

        @blk.vector
        def _(v):
            for i in range(NT):
                p.w(v, "x6l", i + 1)
                ins = v.reduce_sum(stS[:, 0:1], t_st[:, 96 + i * 8:96 + i * 8 + 8], AX.X)
                p.bar(v, ins, "zv")
                v.reduce_sum(stQ[:, 0:1], t_st[:, 224 + i * 8:224 + i * 8 + 8], AX.X)
                ins = v.tensor_scalar(stM[:, 0:1], stS[:, 0:1], 1.0 / DM, None, ALU.mult)
                p.bar(v, ins, "zv")
                if i >= 1:
                    p.w(v, "x6lnv", i)
                ins = v.tensor_tensor(stV[:, 0:1], stM[:, 0:1], stM[:, 0:1], ALU.mult)
                p.bar(v, ins, "zv")
                ins = v.scalar_tensor_tensor(stV[:, 0:1], stQ[:, 0:1], 1.0 / DM,
                                             stV[:, 0:1], ALU.mult, ALU.subtract)
                p.inc(ins, "x6v")
                p.w(v, "x6r", i + 1)
                ins = v.scalar_tensor_tensor(stN[:, 0:1], stM[:, 0:1], -1.0,
                                             stR[:, 0:1], ALU.mult, ALU.mult)
                p.inc(ins, "x6n")
                p.w(v, "x6t", i + 1)
                if i >= 2:
                    p.w(v, "x6sp", i - 1)
                ins = v.tensor_add(x2b[i % 2], xb[i % 2], tmp)
                p.bar(v, ins, "zv")
                ins = v.engine_nop()
                p.inc(ins, "x6x2")
                ins = v.reduce_sum(stS[:, 1:2], x2b[i % 2], AX.X)
                p.bar(v, ins, "zv")
                ins = v.tensor_scalar(stM[:, 1:2], stS[:, 1:2], 1.0 / DM, None, ALU.mult)
                p.bar(v, ins, "zv")
                if i >= 1:
                    p.w(v, "x6lnv2", i)
                ins = v.tensor_tensor(stV[:, 1:2], stM[:, 1:2], stM[:, 1:2], ALU.mult)
                p.bar(v, ins, "zv")
                p.w(v, "x6q2", i + 1)
                ins = v.scalar_tensor_tensor(stV[:, 1:2], stQ[:, 1:2], 1.0 / DM,
                                             stV[:, 1:2], ALU.mult, ALU.subtract)
                p.inc(ins, "x6v2")
                p.w(v, "x6r2", i + 1)
                ins = v.scalar_tensor_tensor(stN[:, 1:2], stM[:, 1:2], -1.0,
                                             stR[:, 1:2], ALU.mult, ALU.mult)
                p.inc(ins, "x6n2")

        @blk.scalar
        def _(s):
            for i in range(NT):
                p.w(s, "x6v", i + 1)
                ins = s.activation(stR[:, 0:1], stV[:, 0:1], AF.Ln, bias=t_eps, scale=1.0)
                p.bar(s, ins, "x6lnv")
                ins = s.activation(stR[:, 0:1], stR[:, 0:1], AF.Exp, scale=-0.5)
                p.inc(ins, "x6r")
                p.w(s, "x6n", i + 1)
                ins = s.activation(tmp, ob[i % 2], AF.Identity, bias=stN[:, 0:1], scale=stR[:, 0:1])
                p.bar(s, ins, "zs")
                ins = s.nop()
                p.inc(ins, "x6t")
                p.w(s, "x6x2", i + 1)
                ins = s.activation(sqb, x2b[i % 2], AF.Square, accum_out=stQ[:, 1:2])
                p.inc(ins, "x6q2")
                p.w(s, "x6v2", i + 1)
                ins = s.activation(stR[:, 1:2], stV[:, 1:2], AF.Ln, bias=t_eps, scale=1.0)
                p.bar(s, ins, "x6lnv2")
                ins = s.activation(stR[:, 1:2], stR[:, 1:2], AF.Exp, scale=-0.5)
                p.inc(ins, "x6r2")
                p.w(s, "x6n2", i + 1)
                if i >= 2:
                    p.w(s, "x6perm", i - 1)
                ins = s.activation(nst[i % 2], x2b[i % 2], AF.Identity,
                                   bias=stN[:, 1:2], scale=stR[:, 1:2])
                p.bar(s, ins, "zs")
                ins = s.dma_start(out=s_x2[i * 128:(i + 1) * 128, :], in_=x2b[i % 2])
                p.inc(ins, "x6sp", 16)
                ins = s.nop()
                p.inc(ins, "x6nn")
                spm[i] = p.n("x6sp")

        @blk.gpsimd
        def _(g):
            for i in range(NT):
                p.w(g, "x6nn", i + 1)
                if i >= 2:
                    p.w(g, "x6spn", ((i - 1)))
                src = nst[i % 2].rearrange("p (i e) -> p e i", e=8)
                ins = g.tensor_copy(pst[i % 2].rearrange("p (e i) -> p e i", e=8), src)
                p.inc(ins, "x6perm")
                p.w(g, "x6perm", i + 1)
                ins = g.dma_start(out=s_x2n[i * 128:(i + 1) * 128, :], in_=pst[i % 2])
                p.inc(ins, "zdg", 16)
                fol(g, "x6spn")

    def p8_ffn():
        x2nT = hT
        P8L = int(os.environ.get("KP8", "9"))
        NOCB = int(os.environ.get("KP8OCB", "8"))
        NTT = int(os.environ.get("KP8TT", str(NT)))
        P8V = os.environ.get("KP8V", "full")
        NOST = P8V == "mm"

        @blk.sync
        def _(sy):
            p.w(sy, "x6spn", (NT))
            p.w(sy, "x6sp", NT)
            for f in range(NC):
                ins = sy.dma_start(out=x2nT[:, f * T:(f + 1) * T],
                                   in_=s_x2n[:, f * 128:(f + 1) * 128], transpose=True)
                p.inc(ins, "x2nT", 16)

        b0 = B.mark()
        B.reset(0)
        wtb = [B.take(NC * 512, BF16) for _ in range(2)]
        B.reset(b0)

        @blk.sync
        def _(sy):
            if P8L < 2:
                return
            p.w(sy, "dqs", (32))
            for ocb in range(NOCB):
                if ocb >= 2 and P8L >= 3:
                    p.w(sy, "fmm", (ocb - 1) * NTT)
                for ch in range(NC):
                    ins = sy.dma_start(out=wtb[ocb % 2][:, ch * 512:(ch + 1) * 512],
                                       in_=s_wdq[ocb * 512:(ocb + 1) * 512,
                                                 ch * 128:(ch + 1) * 128], transpose=True)
                    p.inc(ins, "fwl", 16)

        @blk.gpsimd
        def _(g):
            if P8L < 3:
                return
            for ocb in range(NOCB):
                if ocb >= 1:
                    p.w(g, "fmm", ocb * NTT)
                ins = g.dma_start(out=t_biasb[0:1, :],
                                  in_=d_bias[:, ocb * 512:(ocb + 1) * 512])
                p.inc(ins, "zdg", 16)
                fol(g, "fbl")

        @blk.tensor
        def _(t):
            if P8L < 3:
                return
            p.w(t, "ldcg", 6)
            p.w(t, "x2nT", (NC))
            for ocb in range(NOCB):
                p.w(t, "fwl", NC * (ocb + 1))
                p.w(t, "fbl", ((ocb + 1)))
                for tt in range(NTT):
                    n = ocb * NTT + tt
                    bk = psum[n % 4]
                    if n >= 4 and not NOST and P8V not in ("red",):
                        p.w(t, "fevc", n - 3)
                    elif n >= 4 and P8V == "red":
                        p.w(t, "fst", n - 3)
                    t.matmul(bk[:, 0:512], t_ones[0:1, 0:128],
                             t_biasb[0:1, 0:512], start=True, stop=False)
                    for ch in range(NC):
                        ins = t.matmul(bk[:, 0:512],
                                       x2nT[:, ch * T + tt * 128: ch * T + tt * 128 + 128],
                                       wtb[ocb % 2][:, ch * 512:(ch + 1) * 512],
                                       start=False, stop=(ch == NC - 1))
                    p.inc(ins, "fmm")

        fspm = {}

        @blk.vector
        def _(v):
            if P8L < 3 or NOST:
                return
            for ocb in range(NOCB):
                for tt in range(NTT):
                    n = ocb * NTT + tt
                    p.w(v, "fmm", n + 1)
                    col = 96 + tt * 8 + ocb
                    ins = v.reduce_sum(t_st[:, col:col + 1], psum[n % 4][:, 0:512], AX.X)
                    p.inc(ins, "fst")

        @blk.scalar
        def _(s):
            if P8L < 3 or NOST or P8V == "red":
                return
            stage = wtb[1] if P8V == "cpb" else None
            for ocb in range(NOCB):
                for tt in range(NTT):
                    n = ocb * NTT + tt
                    p.w(s, "fst", n + 1)
                    if P8V == "full" and n >= 2:
                        p.w(s, "fsp", fspm[n - 2])
                    col = 224 + tt * 8 + ocb
                    dst = stage[:, 0:512] if stage is not None else t_fst[n % 2]
                    if P8V == "full":
                        s.activation(dst, psum[n % 4][:, 0:512], AF.Square,
                                     accum_out=t_st[:, col:col + 1])
                    ins = s.copy(dst, psum[n % 4][:, 0:512])
                    p.inc(ins, "fevc")
                    if P8V == "full":
                        p.w(s, "fevc", n + 1)
                        ins = s.dma_start(out=s_ffn[tt * 128:(tt + 1) * 128,
                                                    ocb * 512:(ocb + 1) * 512],
                                          in_=t_fst[n % 2])
                        p.inc(ins, "fsp", 16)
                        fspm[n] = p.n("fsp")

    def p9_out():
        a0 = A.mark()
        A.reset(0)
        xb = [A.take(DM, F32) for _ in range(2)]
        fb = [A.take(DM, BF16) for _ in range(2)]
        tmp = A.take(DM, F32)
        oub = [A.take(DM, F32) for _ in range(2)]
        A.reset(a0)
        stS, stQ = t_st[:, 0:16], t_st[:, 16:32]
        stM, stR = t_st[:, 32:48], t_st[:, 48:64]
        stN, stV = t_st[:, 64:80], t_st[:, 80:96]
        owm = {}

        @blk.sync
        def _(sy):
            p.wall(sy, "fsp")
            for i in range(NT):
                if i >= 2:
                    p.w(sy, "x9o", i - 1)
                ins = sy.dma_start(out=xb[i % 2], in_=s_x2[i * 128:(i + 1) * 128, :])
                p.inc(ins, "zd", 16)
                ins = sy.dma_start(out=fb[i % 2], in_=s_ffn[i * 128:(i + 1) * 128, :])
                p.inc(ins, "zd", 16)
                fol(sy, "x9l")
                if i >= 2:
                    j = i - 2
                    ins = sy.dma_start(out=d_out[j * 128:(j + 1) * 128, :], in_=oub[j % 2])
                    p.inc(ins, "outw", 16)
            for j in (NT - 2, NT - 1):
                p.w(sy, "x9o", j + 1)
                ins = sy.dma_start(out=d_out[j * 128:(j + 1) * 128, :], in_=oub[j % 2])
                p.inc(ins, "outw", 16)
            p.w(sy, "outw", (NT))

        @blk.vector
        def _(v):
            for i in range(NT):
                p.w(v, "x9l", i + 1)
                ins = v.reduce_sum(stS[:, 0:1], t_st[:, 96 + i * 8:96 + i * 8 + 8], AX.X)
                p.bar(v, ins, "zv")
                v.reduce_sum(stQ[:, 0:1], t_st[:, 224 + i * 8:224 + i * 8 + 8], AX.X)
                ins = v.tensor_scalar(stM[:, 0:1], stS[:, 0:1], 1.0 / DM, None, ALU.mult)
                p.bar(v, ins, "zv")
                if i >= 1:
                    p.w(v, "x9lnv", i)
                ins = v.tensor_tensor(stV[:, 0:1], stM[:, 0:1], stM[:, 0:1], ALU.mult)
                p.bar(v, ins, "zv")
                ins = v.scalar_tensor_tensor(stV[:, 0:1], stQ[:, 0:1], 1.0 / DM,
                                             stV[:, 0:1], ALU.mult, ALU.subtract)
                p.inc(ins, "x9v")
                p.w(v, "x9r", i + 1)
                ins = v.scalar_tensor_tensor(stN[:, 0:1], stM[:, 0:1], -1.0,
                                             stR[:, 0:1], ALU.mult, ALU.mult)
                p.inc(ins, "x9n")
                p.w(v, "x9t", i + 1)
                if i >= 2:
                    p.w(v, "outw", owm[i - 2])
                ins = v.tensor_add(oub[i % 2], xb[i % 2], tmp)
                p.bar(v, ins, "zv")
                ins = v.engine_nop()
                p.inc(ins, "x9o")
                owm[i] = i + 1

        @blk.scalar
        def _(s):
            for i in range(NT):
                p.w(s, "x9v", i + 1)
                ins = s.activation(stR[:, 0:1], stV[:, 0:1], AF.Ln, bias=t_eps, scale=1.0)
                p.bar(s, ins, "x9lnv")
                ins = s.activation(stR[:, 0:1], stR[:, 0:1], AF.Exp, scale=-0.5)
                p.inc(ins, "x9r")
                p.w(s, "x9n", i + 1)
                ins = s.activation(tmp, fb[i % 2], AF.Identity, bias=stN[:, 0:1], scale=stR[:, 0:1])
                p.bar(s, ins, "zs")
                ins = s.nop()
                p.inc(ins, "x9t")


    PH = int(os.environ.get("KPHASES", "99"))
    attn = A.t.ap()[:, 0:65536].bitcast(BF16)
    def dbg_attn_dump():
        if "dbg_attn" not in d_dbg:
            return

        @blk.sync
        def _(sy):
            p.w(sy, "p4o", 8 * KVH)
            ins = sy.dma_start(out=d_dbg["dbg_attn"], in_=attn)
            p.inc(ins, "dbgw2", 16)

    def dbg_attn2_dump():
        if "dbg_attn2" not in d_dbg:
            return

        @blk.sync
        def _(sy):
            p.w(sy, "p4o", 16 * KVH)
            ins = sy.dma_start(out=d_dbg["dbg_attn2"], in_=attn)
            p.inc(ins, "dbgw3", 16)

    steps = [
        lambda: (p_ln1(d_xp, "p"), p_hT("p")),
        lambda: p_kvg("p", own=False),
        lambda: (p_ln1(d_x, "o", awaits=[("vmmp", 2 * NT)]), p2_state()),
        lambda: (p_hT("o"), p_kvg("o", own=True)),
        lambda: p3_q(),
        lambda: p4_attn(0, attn, apos=65536),
        lambda: (p5_wo(0, attn), dbg_attn_dump()),
        lambda: (p4_attn(1, attn, apos=65536), p5_wo(1, attn), dbg_attn2_dump()),
        lambda: p7_dequant(apos=114688),
        lambda: p6_x2(),
        lambda: p8_ffn(),
        lambda: p9_out(),
    ]
    for i, st in enumerate(steps):
        if i < PH:
            st()

    if "dbg_S" in d_dbg:
        @blk.sync
        def _(sy):
            p.w(sy, "Sini", 2)
            ins = sy.dma_start(out=d_dbg["dbg_S"], in_=t_S)
            p.inc(ins, "dbgw", 16)
            p.wall(sy, "dbgw")

    p.es.close()
    return p


def _constants(h):
    pos0 = h * T - W
    pos = pos0 + np.arange(2112)
    posc = np.clip(pos, 0, None)
    half = np.arange(64, dtype=np.float64)
    inv = 1.0 / (10000.0 ** (2.0 * half / 128.0))
    fr = posc[None, :] * inv[:, None]
    cos = np.cos(np.concatenate([fr, fr], axis=0))
    sin = np.sin(np.concatenate([fr, fr], axis=0))
    t2 = np.concatenate([-sin[:64], sin[64:]], axis=0)
    sig = lambda M: np.concatenate([M[64:], M[:64]], axis=0)
    mk = lambda M: np.ascontiguousarray(M.astype(np.float32).astype(BF))
    a = np.arange(128)[:, None]
    c = np.arange(192)[None, :]
    rel = a + 64 - c
    base = (rel >= 0) & (rel <= W)
    mi = np.where(base, 0.0, -30000.0).astype(np.float32)
    if h == 0:
        m0 = np.where(base & (c >= 64), 0.0, -30000.0).astype(np.float32)
    else:
        m0 = mi
    tril = (np.arange(128)[:, None] <= np.arange(128)[None, :]).astype(np.float32)
    return dict(
        t1=mk(cos), t2=mk(t2), t1s=mk(sig(cos)), t2s=mk(sig(t2)),
        swam0=np.ascontiguousarray(m0), swami=np.ascontiguousarray(mi),
        tril=np.ascontiguousarray(tril),
        ident=np.eye(128, dtype=np.float32).astype(BF),
        psig=np.ascontiguousarray(np.roll(np.eye(128, dtype=np.float32), 64, axis=0)).astype(BF),
        ones=np.ones((128, 128), np.float32).astype(BF),
        flag=np.ascontiguousarray(np.broadcast_to(np.array([float(h), EPS], np.float32), (128, 2)).copy()),
    )


def kernel(L0_attn_hidden_states, L0_attn_Wq, L0_attn_Wk, L0_attn_Wv, L0_attn_Wo,
           L0_ffn_packed_weight, L0_ffn_scale, L0_ffn_bias):
    x = np.ascontiguousarray(np.asarray(L0_attn_hidden_states, dtype=np.float32))
    wq = np.ascontiguousarray(np.asarray(L0_attn_Wq, dtype=np.float32))
    wk = np.ascontiguousarray(np.asarray(L0_attn_Wk, dtype=np.float32))
    wv = np.ascontiguousarray(np.asarray(L0_attn_Wv, dtype=np.float32))
    wo = np.ascontiguousarray(np.asarray(L0_attn_Wo, dtype=np.float32))
    pk = np.ascontiguousarray(np.asarray(L0_ffn_packed_weight, dtype=np.int32))
    sc = np.ascontiguousarray(np.asarray(L0_ffn_scale, dtype=np.float32))
    bias = np.ascontiguousarray(np.asarray(L0_ffn_bias, dtype=np.float32)).reshape(1, DM)

    prog = build_program()
    consts = [_constants(0), _constants(1)]
    in_maps = []
    for core in range(8):
        b, h = core // 2, core % 2
        in_maps.append(dict(
            x_own=np.ascontiguousarray(x[b, h * T:(h + 1) * T]),
            x_pre=np.ascontiguousarray(x[b, 0:T]),
            wq=wq, wk=wk, wv=wv, wo=wo, pk=pk, sc=sc, fbias=bias,
            **consts[h],
        ))
    res = run_bass_kernel_spmd(prog.nc, in_maps, core_ids=list(range(8)))
    out = np.empty((4, 4096, DM), np.float32)
    for core in range(8):
        b, h = core // 2, core % 2
        out[b, h * T:(h + 1) * T] = res.results[core]["out"]
    return out
